# version 01 (final)
# baseline (speedup 1.0000x reference)
import os
import numpy as np
import ml_dtypes
import concourse.bass as bass
import concourse.mybir as mybir
from concourse.bass_utils import run_bass_kernel_spmd
from contextlib import ExitStack

F32 = mybir.dt.float32
F16 = mybir.dt.float16
BF16 = mybir.dt.bfloat16
AF = mybir.ActivationFunctionType
ALU = mybir.AluOpType
AX = mybir.AxisListType

S = 8192
D = 1024
H = 16
DH = 128
DEXP = 2048
NT = 64
NG = 16
TPC = 16
KC = 512
NCH = 16
KCONST = 24.0
EPS = 1e-5
NEG = -60000.0

ENGS = ("sync", "gpsimd", "scalar", "vector", "tensor")

def ts_mul(e, out, in0, scalar1):
    return e.tensor_scalar(out=out, in0=in0, scalar1=scalar1, scalar2=None,
                           op0=ALU.mult, op1=ALU.bypass)



class Rec:

    def __init__(self, nc, es):
        self.nc = nc
        self.ops = {e: [] for e in ENGS}
        self.cnt = {}
        self.sems = {}
        self.es = es
        self.junk = None
        self._vprev_tiny = False

    def sem(self, name):
        self.sems[name] = self.es.enter_context(self.nc.semaphore(name))
        self.cnt[name] = 0

    def op(self, eng, fn, waits=(), inc=None, amt=None, big=False, nohaz=False):
        if eng == "vector":
            if self._vprev_tiny and self.junk is not None:
                j = self.junk
                self.ops[eng].append((lambda e, j=j: e.memset(j[:], 0.0), (), None, None))
            self._vprev_tiny = not (big or nohaz)
        for s, v in waits:
            assert v <= self.cnt[s], f"wait {s}>={v} but only {self.cnt[s]} planned"
        if inc is not None and amt is None:
            amt = 16 if eng in ("sync", "gpsimd") else 1
        self.ops[eng].append((fn, tuple((s, v) for s, v in waits if v > 0), inc, amt))
        if inc is not None:
            self.cnt[inc] += amt
        return self.cnt.get(inc)

    def phase_barrier(self):
        s = self.op("scalar", lambda e: e.nop(), inc="ss")
        v = self.op("vector", lambda e: e.engine_nop(),
                    waits=[("ss", s)], inc="sv")
        return [("sv", v)]

    def fence(self, eng):
        j = self.junk
        return self.dma(eng, j[:, 2:4], j[:, 0:2])

    def dma(self, eng, out, in_, waits=(), inc=None):
        def fn(e, out=out, in_=in_):
            return e.dma_start(out=out, in_=in_)
        if inc is None:
            inc = "qs" if eng == "sync" else "qg"
        return self.op(eng, fn, waits=waits, inc=inc, amt=16)

    def emit(self, block):
        def mk(eng_name):
            def body(eng):
                for fn, waits, inc, amt in self.ops[eng_name]:
                    for s, v in waits:
                        eng.wait_ge(self.sems[s], v)
                    ins = fn(eng)
                    if inc is not None:
                        assert ins is not None, f"{eng_name} op needs inc, returned None"
                        ins.then_inc(self.sems[inc], amt)
            return body
        for e in ENGS:
            getattr(block, e)(mk(e))


def build_program(debug_phase=0):
    DBGT = int(os.environ.get("DK_DBGT", "0"))
    DBGC = int(os.environ.get("DK_DBGC", "0"))
    nc = bass.Bass("TRN2", target_bir_lowering=False)
    dbg = debug_phase > 0
    last = debug_phase if dbg else 99

    def din(name, shape, dtype=F32):
        return nc.dram_tensor(name, list(shape), dtype, kind="ExternalInput").ap()

    x = din("x", [S, D])
    wr = din("wr", [D, D])
    wk = din("wk", [D, D])
    wv = din("wv", [D, D])
    mur_c = din("mur_c", [128, 8])
    muk_c = din("muk_c", [128, 8])
    wq = din("wq", [D, DEXP])
    wka = din("wka", [D, DEXP])
    wva = din("wva", [D, DEXP])
    wp = din("wp", [D, DEXP])
    wy = din("wy", [D, H])
    bq_cols = din("bq_cols", [128, H])
    bk_cols = din("bk_cols", [128, H])
    bv_bc = din("bv_bc", [128, DEXP])
    bp_bc = din("bp_bc", [128, DEXP])
    by_bc = din("by_bc", [128, H])
    lnw_bc = din("lnw_bc", [128, D])
    lnb_bc = din("lnb_bc", [128, D])
    lnow_bc = din("lnow_bc", [128, D])
    lnob_bc = din("lnob_bc", [128, D])
    smear = din("smear", [1, H])
    logsc = din("logsc", [1, H])
    wout = din("wout", [DEXP, D])
    triu = din("triu", [128, 128])
    eye32 = din("eye32", [128, 128])
    eyebf = din("eyebf", [128, 128], BF16)
    ones_row = din("ones_row", [1, 128])
    diag_mask = din("diag_mask", [128, KC])
    halo_mask = din("halo_mask", [TPC, NCH * KC])
    halo_valid = din("halo_valid", [TPC, 1])
    sel4 = din("sel4", [128, 4])
    e127 = din("e127", [128, 1])
    hselm = din("hselm", [1, 4])
    hohb = din("hohb", [16, H * 128], BF16)

    out = nc.dram_tensor("out", [TPC * 128, D], F32, kind="ExternalOutput").ap()

    def spill(name, shape, dtype):
        kind = "ExternalOutput" if dbg else "Internal"
        return nc.dram_tensor(name, list(shape), dtype, kind=kind).ap()

    x1sp = spill("x1sp", [NT * 128, D], F32)
    x1osp = spill("x1osp", [TPC * 128, D], F32)
    ksp = spill("ksp", [NCH, 128, H * KC], BF16)
    vsp = spill("vsp", [NCH, 4, 128, H * 129], BF16)
    qsp = spill("qsp", [TPC, 128, H * 128], BF16)
    psp = spill("psp", [TPC, 128, DEXP], BF16)
    x2sp = spill("x2sp", [TPC * 128, D], F32)
    xnTsp = spill("xnTsp", [TPC, 128, 8 * 128], BF16)
    posk_sp = spill("posk_sp", [NCH, H, KC], BF16)
    if dbg:
        dbg1 = nc.dram_tensor("dbg1", [128, 8 * D + 32], F32, kind="ExternalOutput").ap()
        dbg_pos = nc.dram_tensor("dbg_pos", [128, max(NT * H, 1056)], F32, kind="ExternalOutput").ap()
        dbg_halo = nc.dram_tensor("dbg_halo", [128, D], F32, kind="ExternalOutput").ap()
        dbg_qh = nc.dram_tensor("dbg_qh", [128, H * 16], F32, kind="ExternalOutput").ap()

    es_root = ExitStack()
    with es_root as es:
        r = Rec(nc, es)
        for s_ in ("qs", "qg", "sv", "ss", "st"):
            r.sem(s_)

        def sb(name, shape, dtype):
            return es.enter_context(nc.sbuf_tensor(name, shape, dtype))

        c_triu = sb("c_triu", [128, 128], F32)
        c_eye32 = sb("c_eye32", [128, 128], F32)
        c_eyebf = sb("c_eyebf", [128, 128], BF16)
        c_ones = sb("c_ones", [1, 128], F32)
        c_murc = sb("c_murc", [128, 8], F32)
        c_mukc = sb("c_mukc", [128, 8], F32)
        c_sel4 = sb("c_sel4", [128, 4], F32)
        c_e127 = sb("c_e127", [128, 1], F32)
        carrow = sb("carrow", [1, H], F32)
        c_hselm = sb("c_hselm", [1, 4], F32)
        c_diag = sb("c_diag", [128, KC], F32)
        c_bqc = sb("c_bqc", [128, H], F32)
        c_bkc = sb("c_bkc", [128, H], F32)
        c_byb = sb("c_byb", [128, H], F32)
        c_smrow = sb("c_smrow", [1, H], F32)
        c_lsrow = sb("c_lsrow", [1, H], F32)
        c_hvald = sb("c_hvald", [16, 1], F32)
        smear_cols = sb("smear_cols", [128, H], F32)
        omsm_cols = sb("omsm_cols", [128, H], F32)
        qsc_cols = sb("qsc_cols", [128, H], F32)
        bqs_cols = sb("bqs_cols", [128, H], F32)
        pos_T = sb("pos_T", [128, NT * H], F32)
        pos_own = sb("pos_own", [128, TPC * H], F32)
        pos_haloT = sb("pos_haloT", [16, H], F32)
        xn1_halo_t = sb("xn1_halo_t", [16, D], F32)
        x1_halo_t = sb("x1_halo_t", [16, D], F32)
        x2_halo_t = sb("x2_halo_t", [16, D], F32)
        u3_halo_t = sb("u3_halo_t", [16, D], F32)
        kcarry = sb("kcarry", [128, H], F32)
        junk = sb("junk", [128, 4], F32)
        r.junk = junk
        stats = sb("stats", [128, 32], F32)
        st16 = sb("st16", [16, 32], F32)
        qhalo = sb("qhalo", [128, H * 16], BF16)
        phalo = sb("phalo", [16, DEXP], BF16)
        xn1T_halo = sb("xn1T_halo", [128, 8 * 16], BF16)
        hmaskb = sb("hmaskb", [16, KC], F32)
        bias16 = sb("bias16", [128, H], F32)
        bias16h = sb("bias16h", [16, H], F32)
        f2b = sb("f2b", [128, H], F32)

        xn1_halo = xn1_halo_t[:]
        x1_halo = x1_halo_t[:]
        x2_halo = x2_halo_t[:]
        u3_halo = u3_halo_t[:]

        def ln_plain(xv_ap, sq_ap, stat, ldwait, extra_s_wait=()):
            v1 = r.op("vector", lambda e: e.reduce_sum(stat[:, 0:1], xv_ap, axis=AX.X), waits=ldwait, inc="sv")
            s1 = r.op("scalar", lambda e: e.activation(out=sq_ap, in_=xv_ap, func=AF.Square, accum_out=stat[:, 1:2]),
                      waits=list(ldwait) + list(extra_s_wait), inc="ss")
            v2 = r.op("vector", lambda e: ts_mul(e, out=stat[:, 2:3], in0=stat[:, 0:1], scalar1=1.0 / D), waits=[("sv", v1)], inc="sv")
            v3 = r.op("vector", lambda e: ts_mul(e, out=stat[:, 3:4], in0=stat[:, 1:2], scalar1=1.0 / D), waits=[("ss", s1)], inc="sv")
            v4 = r.op("vector", lambda e: e.tensor_mul(out=stat[:, 16:17], in0=stat[:, 2:3], in1=stat[:, 2:3]), waits=[("sv", v2)], inc="sv")
            v5 = r.op("vector", lambda e: e.tensor_sub(out=stat[:, 6:7], in0=stat[:, 3:4], in1=stat[:, 16:17]), waits=[("sv", v4), ("sv", v3)], inc="sv")
            v6 = r.op("vector", lambda e: e.tensor_scalar(out=stat[:, 14:15], in0=stat[:, 6:7], scalar1=EPS, scalar2=None, op0=ALU.add, op1=ALU.bypass), waits=[("sv", v5)], inc="sv")
            v7 = r.op("vector", lambda e: e.reciprocal(out=stat[:, 7:8], in_=stat[:, 14:15]), waits=[("sv", v6)], inc="sv")
            s2 = r.op("scalar", lambda e: e.activation(out=stat[:, 4:5], in_=stat[:, 7:8], func=AF.Sqrt), waits=[("sv", v7)], inc="ss")
            v8 = r.op("vector", lambda e: e.tensor_scalar(out=stat[:, 5:6], in0=stat[:, 2:3], scalar1=stat[:, 4:5], scalar2=-1.0, op0=ALU.mult, op1=ALU.mult), waits=[("ss", s2)], inc="sv")
            return s2, v8, v7

        for ap_, tile in [(triu, c_triu), (eye32, c_eye32), (ones_row, c_ones),
                          (mur_c, c_murc), (muk_c, c_mukc), (sel4, c_sel4), (e127, c_e127),
                          (hselm, c_hselm), (diag_mask, c_diag), (bq_cols, c_bqc),
                          (bk_cols, c_bkc), (by_bc, c_byb), (smear, c_smrow),
                          (logsc, c_lsrow), (halo_valid, c_hvald)]:
            r.dma("sync", tile[:], ap_)
        r.dma("sync", c_eyebf[:], eyebf)
        QSC = r.fence("sync")

        _pbc_scope = ExitStack()
        p_bc = _pbc_scope.enter_context(nc.psum_tensor("p_bc", [128, 64], F32))

        r.op("scalar", lambda e: e.activation(out=st16[0:1, 0:H], in_=c_smrow[:], func=AF.Sigmoid), waits=[("qs", QSC)], inc="ss")
        r.op("scalar", lambda e: e.activation(out=st16[0:1, 16:32], in_=c_lsrow[:], func=AF.Exp, scale=-2.0), inc="ss")
        sv0 = r.op("vector", lambda e: ts_mul(e, out=st16[0:1, 16:32], in0=st16[0:1, 16:32], scalar1=float(1.0 / np.sqrt(DH))), waits=[("ss", r.cnt["ss"])], inc="sv")

        def bcast_row(dst, srcrow, n=H):
            t0 = r.op("tensor", lambda e, s=srcrow, nn=n: e.matmul(p_bc[:, 0:nn], c_ones[:], s, start=True, stop=True),
                      waits=[("sv", r.cnt["sv"]), ("ss", r.cnt["ss"])], inc="st")
            return r.op("scalar", lambda e, d=dst, nn=n: e.copy(out=d, in_=p_bc[:, 0:nn]), waits=[("st", t0)], inc="ss")

        bcast_row(smear_cols[:], st16[0:1, 0:H])
        r.op("vector", lambda e: e.tensor_scalar(out=omsm_cols[:], in0=smear_cols[:], scalar1=-1.0, scalar2=1.0, op0=ALU.mult, op1=ALU.add), waits=[("ss", r.cnt["ss"])], inc="sv")
        bcast_row(qsc_cols[:], st16[0:1, 16:32])
        r.op("vector", lambda e: e.tensor_mul(out=bqs_cols[:], in0=c_bqc[:], in1=qsc_cols[:]), waits=[("ss", r.cnt["ss"])], inc="sv")
        r.op("vector", lambda e: ts_mul(e, out=kcarry[:], in0=c_bkc[:], scalar1=-1.0), inc="sv")
        r.op("vector", lambda e: e.memset(xn1_halo_t[:], 0.0), inc="sv")
        r.op("vector", lambda e: e.memset(x1_halo_t[:], 0.0), inc="sv")
        r.op("vector", lambda e: e.memset(x2_halo_t[:], 0.0), inc="sv")
        SVI = r.cnt["sv"]
        _pbc_scope.close()

        with ExitStack() as p1s:
            def sb1(name, shape, dtype):
                return p1s.enter_context(nc.sbuf_tensor(name, shape, dtype))

            w_r = sb1("w_r", [128, 8 * D], BF16)
            w_k = sb1("w_k", [128, 8 * D], BF16)
            w_v = sb1("w_v", [128, 8 * D], BF16)
            xt = [sb1(f"xt{i}", [128, D], F32) for i in range(2)]
            ut = [sb1(f"ut{i}", [128, D], F32) for i in range(2)]
            uprev = sb1("uprev", [128, D], F32)
            dtile = sb1("dtile", [128, D], F32)
            sqsc = sb1("sqsc", [128, D], F32)
            xrT = sb1("xrT", [128, D], BF16)
            xkT = sb1("xkT", [128, D], BF16)
            rga = sb1("rga", [128, D], F32)
            upT = sb1("upT", [128, D], F32)
            karel = sb1("karel", [128, 128], F32)
            kaT = sb1("kaT", [128, D], BF16)
            x1t = [sb1(f"x1t{i}", [128, D], F32) for i in range(2)]
            ucarry = sb1("ucarry", [1, D], F32)

            p_tp = p1s.enter_context(nc.psum_tensor("p_tp", [128, 1024], F32))
            p_tp2 = p1s.enter_context(nc.psum_tensor("p_tp2", [128, 1024], F32))
            p_mm = p1s.enter_context(nc.psum_tensor("p_mm", [128, 1024], F32))
            p_kb = p1s.enter_context(nc.psum_tensor("p_kb", [128, 128], F32))
            r.op("vector", lambda e: e.memset(ucarry[:], 0.0), inc="sv")

            for k in range(8):
                r.dma("gpsimd", w_r[:, k * D:(k + 1) * D], wr[k * 128:(k + 1) * 128, :])
                r.dma("gpsimd", w_k[:, k * D:(k + 1) * D], wk[k * 128:(k + 1) * 128, :])
                r.dma("gpsimd", w_v[:, k * D:(k + 1) * D], wv[k * 128:(k + 1) * 128, :])
            WP1 = r.cnt["qg"]

            rel = {"x": [0, 0], "x1": [0, 0]}
            for T in range(NT):
                b = T % 2
                xv, uv, x1v = xt[b], ut[b], x1t[b]
                ld = r.dma("sync", xv[:], x[T * 128:(T + 1) * 128, :],
                           waits=([("sv", rel["x"][b])] if T >= 2 else []))
                sr, vb_, _v7 = ln_plain(xv[:], sqsc[:], stats, [("qs", ld)], extra_s_wait=[("sv", r.cnt["sv"])])
                su = r.op("scalar", lambda e, xv=xv, uv=uv: e.activation(out=uv[:], in_=xv[:], func=AF.Identity, bias=stats[:, 5:6], scale=stats[:, 4:5]),
                          waits=[("sv", vb_), ("qg", rel.get("u%d" % b, 0))], inc="ss")
                if dbg and last == 1 and T == 0:
                    dbgsb2 = p1s.enter_context(nc.sbuf_tensor("dbgsb2", [128, D + 32], F32))
                    r.op("vector", lambda e, xv=xv: e.tensor_copy(out=dbgsb2[:, 0:D], in_=xv[:]), waits=[("ss", su)], inc="sv")
                    r.op("vector", lambda e: e.tensor_copy(out=dbgsb2[:, D:D + 32], in_=stats[:]), inc="sv")
                g1 = r.dma("gpsimd", uprev[0:1, :], ucarry[:], waits=[("ss", su), ("sv", r.cnt["sv"]), ("st", r.cnt["st"])])
                g2 = r.dma("gpsimd", uprev[1:128, :], uv[0:127, :])
                g3 = r.dma("gpsimd", ucarry[:], uv[127:128, :])
                for _pad in range(3):
                    g3 = r.dma("gpsimd", junk[:], junk[:])
                rel["u%d" % b] = g3
                vd = r.op("vector", lambda e, uv=uv: e.tensor_sub(out=dtile[:], in0=uv[:], in1=uprev[:]),
                          waits=[("qg", g3), ("st", r.cnt["st"])], inc="sv")
                t1 = t2 = None
                for kk in range(8):
                    t1 = r.op("tensor", lambda e, k=kk: e.transpose(p_tp[:, k * 128:(k + 1) * 128], dtile[:, k * 128:(k + 1) * 128], c_eye32[:]),
                              waits=[("sv", vd)] if kk == 0 else [], inc="st" if kk == 7 else None)
                for kk in range(8):
                    t2 = r.op("tensor", lambda e, k=kk: e.transpose(p_tp2[:, k * 128:(k + 1) * 128], uprev[:, k * 128:(k + 1) * 128], c_eye32[:]),
                              inc="st" if kk == 7 else None)
                ve = r.op("vector", lambda e: e.tensor_copy(out=upT[:], in_=p_tp2[:]), waits=[("st", t2)], inc="sv")
                vxr = None
                for kk in range(8):
                    vxr = r.op("vector", lambda e, k=kk: e.scalar_tensor_tensor(out=xrT[:, k * 128:(k + 1) * 128], in0=p_tp[:, k * 128:(k + 1) * 128], scalar=c_murc[:, k:k + 1], in1=upT[:, k * 128:(k + 1) * 128], op0=ALU.mult, op1=ALU.add),
                              waits=[("st", t1)] if kk == 0 else [], inc="sv" if kk == 7 else None)
                vxk = None
                for kk in range(8):
                    vxk = r.op("vector", lambda e, k=kk: e.scalar_tensor_tensor(out=xkT[:, k * 128:(k + 1) * 128], in0=p_tp[:, k * 128:(k + 1) * 128], scalar=c_mukc[:, k:k + 1], in1=upT[:, k * 128:(k + 1) * 128], op0=ALU.mult, op1=ALU.add),
                              inc="sv" if kk == 7 else None)
                mmr = None
                for kk in range(8):
                    for g in range(2):
                        mmr = r.op("tensor", lambda e, k=kk, gg=g: e.matmul(
                            p_mm[:, gg * 512:(gg + 1) * 512], xrT[:, k * 128:(k + 1) * 128],
                            w_r[:, k * D + gg * 512:k * D + (gg + 1) * 512],
                            start=(k == 0), stop=(k == 7)),
                            waits=([("sv", vxr), ("qg", WP1), ("ss", r.cnt["ss"])] if (kk == 0 and g == 0) else []),
                            inc="st" if (kk == 7 and g == 1) else None)
                sg = r.op("scalar", lambda e: e.activation(out=rga[:], in_=p_mm[:], func=AF.Sigmoid),
                          waits=[("st", mmr), ("sv", vxk)], inc="ss")
                for blk in range(8):
                    mk_ = None
                    for kk in range(8):
                        mk_ = r.op("tensor", lambda e, k=kk, b2=blk: e.matmul(
                            p_kb[:], w_k[:, k * D + b2 * 128:k * D + (b2 + 1) * 128],
                            xkT[:, k * 128:(k + 1) * 128],
                            start=(k == 0), stop=(k == 7)),
                            waits=([("sv", vxk), ("ss", sg)] if (kk == 0 and blk == 0) else ([("sv", r.cnt["sv"])] if kk == 0 else [])),
                            inc="st" if kk == 7 else None)
                    sre = r.op("scalar", lambda e: e.activation(out=karel[:], in_=p_kb[:], func=AF.Relu),
                               waits=[("st", mk_), ("sv", r.cnt["sv"])], inc="ss")
                    r.op("vector", lambda e, b2=blk: e.tensor_mul(out=kaT[:, b2 * 128:(b2 + 1) * 128], in0=karel[:], in1=karel[:]),
                         waits=[("ss", sre)], inc="sv")
                vka = r.cnt["sv"]
                mmv = None
                for kk in range(8):
                    for g in range(2):
                        mmv = r.op("tensor", lambda e, k=kk, gg=g: e.matmul(
                            p_mm[:, gg * 512:(gg + 1) * 512], kaT[:, k * 128:(k + 1) * 128],
                            w_v[:, k * D + gg * 512:k * D + (gg + 1) * 512],
                            start=(k == 0), stop=(k == 7)),
                            waits=[("sv", vka)] if (kk == 0 and g == 0) else [],
                            inc="st" if (kk == 7 and g == 1) else None)
                vm = r.op("vector", lambda e: e.tensor_mul(out=sqsc[:], in0=rga[:], in1=p_mm[:]),
                          waits=[("st", mmv), ("ss", r.cnt["ss"])], inc="sv")
                vx1 = r.op("vector", lambda e, xv=xv, x1v=x1v: e.scalar_tensor_tensor(out=x1v[:], in0=sqsc[:], scalar=0.5, in1=xv[:], op0=ALU.mult, op1=ALU.add),
                           waits=[("qg", rel["x1"][b])], inc="sv")
                rel["x"][b] = vx1
                sp = r.dma("gpsimd", x1sp[T * 128:(T + 1) * 128, :], x1v[:], waits=[("sv", vx1)])
                rel["x1"][b] = sp
                if dbg and last == 1 and T == 0:
                    dbgsb = p1s.enter_context(nc.sbuf_tensor("dbgsb", [128, 8 * D + 32], F32))
                    pairs = [(uv[:], 0), (uprev[:], 1), (xrT[:], 2), (rga[:], 3),
                             (sqsc[:], 4), (upT[:], 5), (kaT[:], 6), (x1v[:], 7)]
                    vdbg = None
                    for src_, idx in pairs:
                        vdbg = r.op("vector", lambda e, s_=src_, i_=idx: e.tensor_copy(out=dbgsb[:, i_ * D:(i_ + 1) * D], in_=s_),
                                    waits=[("sv", vx1)] if idx == 0 else [], inc="sv")
                    vdbg = r.op("vector", lambda e: e.tensor_copy(out=dbgsb[:, 8 * D:8 * D + 32], in_=stats[:]), inc="sv")
                    r.dma("gpsimd", dbg1, dbgsb[:], waits=[("sv", vdbg)])
                    r.dma("gpsimd", dbg_pos[:, 0:1024], dbgsb2[:, 0:1024], waits=[("sv", vdbg)])
                    r.dma("gpsimd", dbg_qh[:, 0:32], dbgsb2[:, D:D + 32], waits=[("sv", vdbg)])

        QG_P1 = r.cnt["qg"]
        BAR_P1 = r.phase_barrier() if last >= 2 else []
        if last >= 2:
            with ExitStack() as p2s:
                def sb2(name, shape, dtype):
                    return p2s.enter_context(nc.sbuf_tensor(name, shape, dtype))

                w_ka = sb2("w_ka", [128, 8 * DEXP], BF16)
                w_va = sb2("w_va", [128, 8 * DEXP], BF16)
                w_y = sb2("w_y", [128, 8 * H], BF16)
                x1b = [sb2(f"x1b{i}", [128, D], F32) for i in range(2)]
                xnb = sb2("xnb", [128, D], F32)
                xnTg = sb2("xnTg", [128, 8 * KC], BF16)
                sigT = sb2("sigT", [128, H], F32)
                vbuf = [sb2(f"vbuf{i}", [128, DEXP], BF16) for i in range(4)]
                ksm = sb2("ksm", [128, KC], F32)
                kst = sb2("kst", [128, KC], BF16)
                vstg = sb2("vstg", [128, H * 129], BF16)
                ownxn = sb2("ownxn", [128, D], F32)
                ownx1 = sb2("ownx1", [128, D], F32)
                ftile = sb2("ftile", [128, H], F32)
                ring = sb2("ring", [1, 4 * D], F32)
                xring = sb2("xring", [1, 4 * D], F32)
                posring = sb2("posring", [1, 4 * H], F32)
                selrow = sb2("selrow", [1, D], F32)
                posb16 = sb2("posb16", [128, 4 * H], BF16)
                posb32 = sb2("posb32", [128, 4 * H], F32)
                pkstage = sb2("pkstage", [16, KC], BF16)
                oxnT = sb2("oxnT", [128, 8 * 128], BF16)
                c_bvb = sb2("c_bvb", [128, DEXP], F32)
                c_lnw = sb2("c_lnw", [128, D], F32)
                c_lnb = sb2("c_lnb", [128, D], F32)

                p_tp = p2s.enter_context(nc.psum_tensor("p2_tp", [128, 1024], F32))
                p_v = p2s.enter_context(nc.psum_tensor("p2_v", [128, 2048], F32))
                p_sm = p2s.enter_context(nc.psum_tensor("p2_sm", [128, 64], F32))
                p_k = p2s.enter_context(nc.psum_tensor("p2_k", [128, 512], F32))

                for k in range(8):
                    r.dma("gpsimd", w_ka[:, k * DEXP:(k + 1) * DEXP], wka[k * 128:(k + 1) * 128, :], waits=BAR_P1 if k == 0 else [])
                    r.dma("gpsimd", w_va[:, k * DEXP:(k + 1) * DEXP], wva[k * 128:(k + 1) * 128, :])
                    r.dma("gpsimd", w_y[:, k * H:(k + 1) * H], wy[k * 128:(k + 1) * 128, :])
                r.dma("sync", c_bvb[:], bv_bc, waits=BAR_P1)
                r.dma("sync", c_lnw[:], lnw_bc)
                r.dma("sync", c_lnb[:], lnb_bc)
                WP2 = r.cnt["qg"]
                QS2 = r.cnt["qs"]
                r.op("vector", lambda e: e.memset(ring[:], 0.0), waits=[("qs", QS2)], inc="sv")
                r.op("vector", lambda e: e.memset(xring[:], 0.0), inc="sv")
                r.op("vector", lambda e: e.memset(posring[:], 0.0), inc="sv")

                rel2 = {"x1b": [0, 0], "vbuf": [0, 0, 0, 0], "own": 0}
                for g in range(NG):
                    for tl in range(4):
                        T = 4 * g + tl
                        b = T % 2
                        xv = x1b[b]
                        ld = r.dma("sync", xv[:], x1sp[T * 128:(T + 1) * 128, :],
                                   waits=([("sv", rel2["x1b"][b]), ("qg", rel2.get("stashf", 0))] if T >= 2 else list(BAR_P1)) + [("qg", QG_P1)])
                        s1, _vb0, v7 = ln_plain(xv[:], xnb[:], stats, [("qs", ld)], extra_s_wait=[("sv", r.cnt["sv"]), ("qg", rel2.get("stashf", 0))])
                        v9 = r.op("vector", lambda e: e.tensor_mul(out=stats[:, 8:9], in0=stats[:, 6:7], in1=stats[:, 7:8]), waits=[("sv", v7)], inc="sv")
                        v10 = r.op("vector", lambda e: e.tensor_scalar(out=stats[:, 15:16], in0=stats[:, 8:9], scalar1=EPS, scalar2=None, op0=ALU.add, op1=ALU.bypass), waits=[("sv", v9)], inc="sv")
                        v11 = r.op("vector", lambda e: e.reciprocal(out=stats[:, 10:11], in_=stats[:, 15:16]), waits=[("sv", v10)], inc="sv")
                        s2 = r.op("scalar", lambda e: e.activation(out=stats[:, 11:12], in_=stats[:, 10:11], func=AF.Sqrt), waits=[("sv", v11)], inc="ss")
                        vs_ = r.op("vector", lambda e: e.tensor_mul(out=stats[:, 12:13], in0=stats[:, 4:5], in1=stats[:, 11:12]), waits=[("ss", s2)], inc="sv")
                        vbb = r.op("vector", lambda e: e.tensor_scalar(out=stats[:, 13:14], in0=stats[:, 2:3], scalar1=stats[:, 12:13], scalar2=-1.0, op0=ALU.mult, op1=ALU.mult), waits=[("sv", vs_)], inc="sv")
                        sxn = r.op("scalar", lambda e, xv=xv: e.activation(out=xnb[:], in_=xv[:], func=AF.Identity, bias=stats[:, 13:14], scale=stats[:, 12:13]),
                                   waits=[("sv", vbb), ("st", r.cnt["st"]), ("sv", rel2.get("sel", 0)), ("qg", rel2.get("stashf", 0))], inc="ss")
                        vw = r.op("vector", lambda e: e.tensor_mul(out=xnb[:], in0=xnb[:], in1=c_lnw[:]), waits=[("ss", sxn)], inc="sv")
                        vxn = r.op("vector", lambda e: e.tensor_add(out=xnb[:], in0=xnb[:], in1=c_lnb[:]), inc="sv")
                        tt = None
                        for kk in range(8):
                            tt = r.op("tensor", lambda e, k=kk: e.transpose(p_tp[:, k * 128:(k + 1) * 128], xnb[:, k * 128:(k + 1) * 128], c_eye32[:]),
                                      waits=[("sv", vxn)] if kk == 0 else [], inc="st" if kk == 7 else None)
                        vev = r.op("vector", lambda e, t2_=tl: e.tensor_copy(
                            out=xnTg[:].rearrange("p (k t c) -> p k t c", k=8, t=4)[:, :, t2_, :],
                            in_=p_tp[:].rearrange("p (k c) -> p k c", k=8)),
                            waits=[("st", tt)], inc="sv")
                        my = None
                        for kk in range(8):
                            my = r.op("tensor", lambda e, k=kk, t2_=tl: e.matmul(
                                p_sm[:, 0:H], xnTg[:, k * KC + t2_ * 128:k * KC + (t2_ + 1) * 128],
                                w_y[:, k * H:(k + 1) * H], start=(k == 0), stop=(k == 7)),
                                waits=([("sv", vev), ("qg", WP2), ("ss", r.cnt["ss"])] if kk == 0 else []),
                                inc="st" if kk == 7 else None)
                        vy = r.op("vector", lambda e: e.scalar_tensor_tensor(out=sigT[:], in0=p_sm[:, 0:H], scalar=1.0, in1=c_byb[:], op0=ALU.mult, op1=ALU.add),
                                  waits=[("st", my)], inc="sv")
                        sy = r.op("scalar", lambda e: e.activation(out=sigT[:], in_=sigT[:], func=AF.Sigmoid), waits=[("sv", vy)], inc="ss")
                        mp = r.op("tensor", lambda e, last_=(T == 0): e.matmul(p_sm[:, 16:32], c_triu[:], sigT[:], start=True, stop=last_),
                                  waits=[("ss", sy), ("ss", r.cnt["ss"])], inc="st" if T == 0 else None)
                        if T > 0:
                            mp = r.op("tensor", lambda e: e.matmul(p_sm[:, 16:32], c_ones[:], carrow[:], start=False, stop=True), inc="st")
                        spos = r.op("scalar", lambda e, T_=T: e.copy(out=pos_T[:, T_ * H:(T_ + 1) * H], in_=p_sm[:, 16:32]), waits=[("st", mp)], inc="ss")
                        mcy = r.op("tensor", lambda e, T_=T: e.matmul(p_sm[0:1, 48:64], c_e127[:], pos_T[:, T_ * H:(T_ + 1) * H], start=True, stop=True), waits=[("ss", spos)], inc="st")
                        scy = r.op("scalar", lambda e: e.copy(out=carrow[:], in_=p_sm[0:1, 48:64]), waits=[("st", mcy)], inc="ss")
                        if tl == 0:
                            r.op("vector", lambda e, T_=T, g_=g: ts_mul(e, out=pos_own[:, g_ * H:(g_ + 1) * H], in0=pos_T[:, T_ * H:(T_ + 1) * H], scalar1=c_sel4[:, 0:1]),
                                 waits=[("ss", spos)], inc="sv")
                        else:
                            r.op("vector", lambda e, T_=T, g_=g, t2_=tl: e.scalar_tensor_tensor(out=pos_own[:, g_ * H:(g_ + 1) * H], in0=pos_T[:, T_ * H:(T_ + 1) * H], scalar=c_sel4[:, t2_:t2_ + 1], in1=pos_own[:, g_ * H:(g_ + 1) * H], op0=ALU.mult, op1=ALU.add),
                                 waits=[("ss", spos)], inc="sv")
                        mv = None
                        for kk in range(8):
                            for gg in range(4):
                                mv = r.op("tensor", lambda e, k=kk, g2=gg, t2_=tl: e.matmul(
                                    p_v[:, g2 * 512:(g2 + 1) * 512],
                                    xnTg[:, k * KC + t2_ * 128:k * KC + (t2_ + 1) * 128],
                                    w_va[:, k * DEXP + g2 * 512:k * DEXP + (g2 + 1) * 512],
                                    start=(k == 0), stop=(k == 7)),
                                    waits=([("sv", r.cnt["sv"]), ("ss", spos)] if (kk == 0 and gg == 0) else []),
                                    inc="st" if (kk == 7 and gg == 3) else None)
                        vvv = None
                        for gg in range(4):
                            vvv = r.op("vector", lambda e, g2=gg, t2_=tl: e.scalar_tensor_tensor(
                                out=vbuf[t2_][:, g2 * 512:(g2 + 1) * 512], in0=p_v[:, g2 * 512:(g2 + 1) * 512],
                                scalar=1.0, in1=c_bvb[:, g2 * 512:(g2 + 1) * 512], op0=ALU.mult, op1=ALU.add),
                                waits=([("st", mv), ("qg", rel2["vbuf"][tl])] if gg == 0 else []), inc="sv" if gg == 3 else None)
                        selw = [("sv", r.cnt["sv"]), ("qg", rel2["own"])]

                        if tl == 0:
                            r.op("vector", lambda e: e.tensor_scalar(out=ownxn[:], in0=xnb[:], scalar1=c_sel4[:, 0:1], scalar2=None, op0=ALU.mult, op1=ALU.bypass), waits=selw, inc="sv")
                            r.op("vector", lambda e, xv=xv: e.tensor_scalar(out=ownx1[:], in0=xv[:], scalar1=c_sel4[:, 0:1], scalar2=None, op0=ALU.mult, op1=ALU.bypass), inc="sv")
                        else:
                            r.op("vector", lambda e, t2_=tl: e.scalar_tensor_tensor(out=ownxn[:], in0=xnb[:], scalar=c_sel4[:, t2_:t2_ + 1], in1=ownxn[:], op0=ALU.mult, op1=ALU.add), waits=selw, inc="sv")
                            r.op("vector", lambda e, t2_=tl, xv=xv: e.scalar_tensor_tensor(out=ownx1[:], in0=xv[:], scalar=c_sel4[:, t2_:t2_ + 1], in1=ownx1[:], op0=ALU.mult, op1=ALU.add), inc="sv")
                        vsel = r.cnt["sv"]
                        rel2["sel"] = vsel
                        r.dma("gpsimd", ring[0:1, (T % 4) * D:(T % 4 + 1) * D], xnb[127:128, :], waits=[("sv", vsel)])
                        r.dma("gpsimd", xring[0:1, (T % 4) * D:(T % 4 + 1) * D], xv[127:128, :])
                        r.dma("gpsimd", posring[0:1, (T % 4) * H:(T % 4 + 1) * H], pos_T[127:128, T * H:(T + 1) * H], waits=[("ss", scy)])
                        stashf = r.fence("gpsimd")
                        rel2["stashf"] = stashf
                        rel2["x1b"][b] = vsel
                        if tl == 2:
                            fs = r.fence("gpsimd")
                            vh1 = r.op("vector", lambda e: ts_mul(e, out=selrow[0:1, :], in0=ring[0:1, 0:D], scalar1=c_hselm[0:1, 0:1]),
                                       waits=[("qg", fs)], inc="sv")
                            for rr in range(1, 4):
                                vh1 = r.op("vector", lambda e, rr_=rr: e.scalar_tensor_tensor(
                                    out=selrow[0:1, :], in0=ring[0:1, rr_ * D:(rr_ + 1) * D],
                                    scalar=c_hselm[0:1, rr_:rr_ + 1], in1=selrow[0:1, :],
                                    op0=ALU.mult, op1=ALU.add), inc="sv")
                            g4 = r.dma("gpsimd", xn1_halo[g:g + 1, :], selrow[0:1, :], waits=[("sv", vh1)])
                            vh2 = r.op("vector", lambda e: ts_mul(e, out=selrow[0:1, :], in0=xring[0:1, 0:D], scalar1=c_hselm[0:1, 0:1]),
                                       waits=[("qg", g4)], inc="sv")
                            for rr in range(1, 4):
                                vh2 = r.op("vector", lambda e, rr_=rr: e.scalar_tensor_tensor(
                                    out=selrow[0:1, :], in0=xring[0:1, rr_ * D:(rr_ + 1) * D],
                                    scalar=c_hselm[0:1, rr_:rr_ + 1], in1=selrow[0:1, :],
                                    op0=ALU.mult, op1=ALU.add), inc="sv")
                            g5 = r.dma("gpsimd", x1_halo[g:g + 1, :], selrow[0:1, :], waits=[("sv", vh2)])
                            vh3 = r.op("vector", lambda e: ts_mul(e, out=selrow[0:1, 0:H], in0=posring[0:1, 0:H], scalar1=c_hselm[0:1, 0:1]),
                                       waits=[("qg", g5)], inc="sv")
                            for rr in range(1, 4):
                                vh3 = r.op("vector", lambda e, rr_=rr: e.scalar_tensor_tensor(
                                    out=selrow[0:1, 0:H], in0=posring[0:1, rr_ * H:(rr_ + 1) * H],
                                    scalar=c_hselm[0:1, rr_:rr_ + 1], in1=selrow[0:1, 0:H],
                                    op0=ALU.mult, op1=ALU.add), inc="sv")
                            g6 = r.dma("gpsimd", pos_haloT[g:g + 1, :], selrow[0:1, 0:H], waits=[("sv", vh3)])
                            rel2["own_h"] = r.fence("gpsimd")
                    for h in range(H):
                        mk_ = None
                        for kk in range(8):
                            mk_ = r.op("tensor", lambda e, k=kk, h_=h: e.matmul(
                                p_k[:], w_ka[:, k * DEXP + h_ * 128:k * DEXP + (h_ + 1) * 128],
                                xnTg[:, k * KC:(k + 1) * KC],
                                start=(k == 0), stop=(k == 7)),
                                waits=([("ss", r.cnt["ss"]), ("sv", r.cnt["sv"])] if (kk == 0) else []),
                                inc="st" if kk == 7 else None)
                        v1 = r.op("vector", lambda e, h_=h: e.tensor_scalar(out=ksm[:], in0=p_k[:], scalar1=omsm_cols[:, h_:h_ + 1], scalar2=None, op0=ALU.mult, op1=ALU.bypass),
                                  waits=[("st", mk_), ("ss", rel2.get("sk", 0))], inc="sv")
                        v2 = r.op("vector", lambda e, h_=h: e.scalar_tensor_tensor(out=ksm[:, 1:KC], in0=p_k[:, 0:KC - 1], scalar=smear_cols[:, h_:h_ + 1], in1=ksm[:, 1:KC], op0=ALU.mult, op1=ALU.add), inc="sv")
                        v3 = r.op("vector", lambda e, h_=h: e.scalar_tensor_tensor(out=ksm[:, 0:1], in0=kcarry[:, h_:h_ + 1], scalar=smear_cols[:, h_:h_ + 1], in1=ksm[:, 0:1], op0=ALU.mult, op1=ALU.add), inc="sv")
                        v4 = r.op("vector", lambda e, h_=h: e.tensor_copy(out=kcarry[:, h_:h_ + 1], in_=p_k[:, KC - 1:KC]), inc="sv")
                        sk = r.op("scalar", lambda e, h_=h: e.activation(out=kst[:], in_=ksm[:], func=AF.Identity, bias=c_bkc[:, h_:h_ + 1]),
                                  waits=[("sv", r.cnt["sv"]), ("qg", rel2.get("kst", 0))], inc="ss")
                        rel2["sk"] = sk
                        gk = r.dma("gpsimd", ksp[g][:, h * KC:(h + 1) * KC], kst[:], waits=[("ss", sk)])
                        rel2["kst"] = gk
                    for tl in range(4):
                        T = 4 * g + tl
                        r.op("vector", lambda e, T_=T, t2_=tl: e.tensor_copy(out=posb16[:, t2_ * H:(t2_ + 1) * H], in_=pos_T[:, T_ * H:(T_ + 1) * H]),
                             waits=[("ss", r.cnt["ss"]), ("st", rel2.get("pkT", 0)), ("qg", rel2.get("vstg", 0))], inc="sv")
                        r.op("vector", lambda e, t2_=tl: e.tensor_copy(out=posb32[:, t2_ * H:(t2_ + 1) * H], in_=posb16[:, t2_ * H:(t2_ + 1) * H]), inc="sv")
                        vf = r.op("vector", lambda e, T_=T, t2_=tl: e.tensor_sub(out=ftile[:], in0=pos_T[:, T_ * H:(T_ + 1) * H], in1=posb32[:, t2_ * H:(t2_ + 1) * H]),
                                  inc="sv")
                        sf = r.op("scalar", lambda e: e.activation(out=f2b[:], in_=ftile[:], func=AF.Exp), waits=[("sv", vf)], inc="ss")
                        vst = None
                        for h in range(H):
                            r.op("vector", lambda e, h_=h, t2_=tl: e.tensor_scalar(
                                out=vstg[:, h_ * 129:h_ * 129 + 128], in0=vbuf[t2_][:, h_ * 128:(h_ + 1) * 128],
                                scalar1=f2b[:, h_:h_ + 1], scalar2=None, op0=ALU.mult, op1=ALU.bypass),
                                waits=[("ss", sf)] if h == 0 else [], inc=None if h < H - 1 else None)
                            vst = r.op("vector", lambda e, h_=h: e.tensor_copy(out=vstg[:, h_ * 129 + 128:h_ * 129 + 129], in_=f2b[:, h_:h_ + 1]), inc="sv")
                        gv = r.dma("gpsimd", vsp[g][tl], vstg[:], waits=[("sv", vst)])
                        gvf = r.fence("gpsimd")
                        if dbg and last == 3 and g == 0 and tl == 0:
                            r.dma("gpsimd", dbg1[:, 4608:4624], ftile[:], waits=[("sv", vst)])
                            r.dma("gpsimd", dbg1[:, 4624:4640], f2b[:])
                            r.dma("gpsimd", dbg1[:, 4640:4704], posb32[:])
                            r.dma("gpsimd", dbg1[:, 4704:4768], pos_T[:, 0:64])
                            r.dma("gpsimd", dbg1[:, 4768:4897], vstg[:, 0:129])
                            r.dma("gpsimd", dbg1[:, 4897:5025], vbuf[0][:, 0:128])
                            gvf = r.fence("gpsimd")
                        rel2["vstg"] = gvf
                        rel2["vbuf"][tl] = gvf
                    tpk = None
                    for tl2 in range(4):
                        tpk = r.op("tensor", lambda e, t2_=tl2: e.transpose(p_tp[0:16, t2_ * 128:(t2_ + 1) * 128], posb32[:, t2_ * H:(t2_ + 1) * H], c_eye32[:]),
                                   waits=[("sv", r.cnt["sv"]), ("ss", r.cnt["ss"])] if tl2 == 0 else [],
                                   inc="st" if tl2 == 3 else None)
                    vpk = r.op("vector", lambda e: e.tensor_copy(out=pkstage[:], in_=p_tp[0:16, 0:512]), waits=[("st", tpk), ("qg", rel2.get("pkg", 0))], inc="sv")
                    gpk = r.dma("gpsimd", posk_sp[g], pkstage[:], waits=[("sv", vpk)])
                    rel2["pkT"] = tpk
                    rel2["pkg"] = r.fence("gpsimd")
                    to = None
                    for kk in range(8):
                        to = r.op("tensor", lambda e, k=kk: e.transpose(p_tp[:, k * 128:(k + 1) * 128], ownxn[:, k * 128:(k + 1) * 128], c_eye32[:]),
                                  waits=[("sv", r.cnt["sv"])] if kk == 0 else [], inc="st" if kk == 7 else None)
                    vto = r.op("vector", lambda e: e.tensor_copy(out=oxnT[:], in_=p_tp[:]),
                        waits=[("st", to), ("qg", rel2.get("oxnT", 0))], inc="sv")
                    gxT = r.dma("gpsimd", xnTsp[g], oxnT[:], waits=[("sv", vto)])
                    rel2["oxnT"] = gxT
                    go = r.dma("gpsimd", x1osp[g * 128:(g + 1) * 128, :], ownx1[:], waits=[("sv", vto)])
                    rel2["own"] = go

                hw = [("qg", rel2["own_h"]), ("sv", r.cnt["sv"])]
                th = None
                for kk in range(8):
                    th = r.op("tensor", lambda e, k=kk: e.transpose(p_tp[:, k * 128:k * 128 + 16], xn1_halo[:, k * 128:(k + 1) * 128], c_eye32[0:16, 0:16]),
                              waits=hw if kk == 0 else [], inc="st" if kk == 7 else None)
                r.op("vector", lambda e: e.tensor_copy(
                    out=xn1T_halo[:].rearrange("p (k c) -> p k c", k=8),
                    in_=p_tp[:].rearrange("p (k c) -> p k c", k=8)[:, :, 0:16]),
                    waits=[("st", th)], inc="sv")

            BAR_P2A = r.phase_barrier() if not os.environ.get("DK_NOBAR2") else []
            with ExitStack() as pqs:
                def sbq(name, shape, dtype):
                    return pqs.enter_context(nc.sbuf_tensor(name, shape, dtype))
                w_q = sbq("w_q", [128, 8 * DEXP], BF16)
                w_p = sbq("w_p", [128, 8 * DEXP], BF16)
                xn1T_own = sbq("xn1T_own", [128, 8 * TPC * 128], BF16)
                c_bpb = sbq("c_bpb", [128, DEXP], F32)
                qstg = sbq("qstg", [128, KC], BF16)
                pstg = sbq("pstg", [128, DEXP], F32)
                pbf = sbq("pbf", [128, DEXP], BF16)
                p_q = pqs.enter_context(nc.psum_tensor("pq_q", [128, 512], F32))
                p_p = pqs.enter_context(nc.psum_tensor("pq_p", [128, 2048], F32))
                p_h = pqs.enter_context(nc.psum_tensor("pq_h", [128, 512], F32))

                for k in range(8):
                    r.dma("gpsimd", w_q[:, k * DEXP:(k + 1) * DEXP], wq[k * 128:(k + 1) * 128, :], waits=BAR_P2A if k == 0 else [])
                    r.dma("gpsimd", w_p[:, k * DEXP:(k + 1) * DEXP], wp[k * 128:(k + 1) * 128, :])
                r.dma("sync", c_bpb[:], bp_bc, waits=BAR_P2A)
                for g_ in range(TPC):
                    r.dma("sync", xn1T_own[:].rearrange("p (k t c) -> p k t c", k=8, t=TPC)[:, :, g_, :],
                          xnTsp[g_].rearrange("p (k c) -> p k c", k=8), waits=([("qg", rel2["oxnT"])] + BAR_P2A) if g_ == 0 else [])
                WPQ = r.cnt["qg"]
                QSQ = r.fence("sync")
                relq = {"q": 0, "p": 0}
                for h in range(H):
                    for tg in range(4):
                        mq = None
                        for kk in range(8):
                            mq = r.op("tensor", lambda e, k=kk, h_=h, t_=tg: e.matmul(
                                p_q[:], w_q[:, k * DEXP + h_ * 128:k * DEXP + (h_ + 1) * 128],
                                xn1T_own[:, k * 2048 + t_ * 512:k * 2048 + (t_ + 1) * 512],
                                start=(k == 0), stop=(k == 7)),
                                waits=([("qg", WPQ), ("qs", QSQ), ("ss", r.cnt["ss"]), ("sv", r.cnt["sv"])] if (h == 0 and tg == 0 and kk == 0) else ([("ss", relq["q"])] if kk == 0 else [])),
                                inc="st" if kk == 7 else None)
                        sq_ = r.op("scalar", lambda e, h_=h: e.activation(out=qstg[:], in_=p_q[:], func=AF.Identity, bias=bqs_cols[:, h_:h_ + 1], scale=qsc_cols[:, h_:h_ + 1]),
                                   waits=[("st", mq), ("qg", relq["p"])], inc="ss")
                        relq["q"] = sq_
                        for tt_ in range(4):
                            gq = r.dma("gpsimd", qsp[tg * 4 + tt_][:, h * 128:(h + 1) * 128], qstg[:, tt_ * 128:(tt_ + 1) * 128], waits=[("ss", sq_)])
                        relq["p"] = gq
                for h in range(H):
                    mq = None
                    for kk in range(8):
                        mq = r.op("tensor", lambda e, k=kk, h_=h: e.matmul(
                            p_h[:, 0:16], w_q[:, k * DEXP + h_ * 128:k * DEXP + (h_ + 1) * 128],
                            xn1T_halo[:, k * 16:(k + 1) * 16],
                            start=(k == 0), stop=(k == 7)),
                            waits=[("ss", r.cnt["ss"])] if kk == 0 else [], inc="st" if kk == 7 else None)
                    r.op("scalar", lambda e, h_=h: e.activation(out=qhalo[:, h_ * 16:(h_ + 1) * 16], in_=p_h[:, 0:16], func=AF.Identity, bias=bqs_cols[:, h_:h_ + 1], scale=qsc_cols[:, h_:h_ + 1]),
                         waits=[("st", mq)], inc="ss")
                for m in range(TPC):
                    mp_ = None
                    for kk in range(8):
                        for gg in range(4):
                            mp_ = r.op("tensor", lambda e, k=kk, g2=gg, m_=m: e.matmul(
                                p_p[:, g2 * 512:(g2 + 1) * 512],
                                xn1T_own[:, k * 2048 + m_ * 128:k * 2048 + (m_ + 1) * 128],
                                w_p[:, k * DEXP + g2 * 512:k * DEXP + (g2 + 1) * 512],
                                start=(k == 0), stop=(k == 7)),
                                waits=[("sv", r.cnt["sv"]), ("ss", r.cnt["ss"])] if (kk == 0 and gg == 0) else [],
                                inc="st" if (kk == 7 and gg == 3) else None)
                    r.op("vector", lambda e: e.tensor_add(out=pstg[:, 0:1024], in0=p_p[:, 0:1024], in1=c_bpb[:, 0:1024]), waits=[("st", mp_), ("ss", relq.get("sp", 0))], big=True, inc="sv")
                    vp = r.op("vector", lambda e: e.tensor_add(out=pstg[:, 1024:2048], in0=p_p[:, 1024:2048], in1=c_bpb[:, 1024:2048]), big=True, inc="sv")
                    sp_ = r.op("scalar", lambda e: e.activation(out=pbf[:], in_=pstg[:], func=AF.Silu), waits=[("sv", vp), ("qg", relq.get("pb", 0))], inc="ss")
                    relq["sp"] = sp_
                    if dbg and last == 2 and m == 0:
                        dbgp = pqs.enter_context(nc.sbuf_tensor("dbgp", [128, 2 * D], F32))
                        vdp = r.op("vector", lambda e: e.tensor_copy(out=dbgp[:, 0:D], in_=p_p[:, 1024:2048]), waits=[("ss", sp_)], inc="sv")
                        vdp = r.op("vector", lambda e: e.tensor_copy(out=dbgp[:, D:2 * D], in_=pstg[:, 1024:2048]), inc="sv")
                        r.dma("gpsimd", dbg1[:, 0:2 * D], dbgp[:], waits=[("sv", vdp)])
                        fdp = r.fence("gpsimd")
                        vdp = r.op("vector", lambda e: e.tensor_copy(out=dbgp[:, 0:D], in_=c_bpb[:, 1024:2048]), waits=[("qg", fdp)], inc="sv")
                        vdp = r.op("vector", lambda e: e.tensor_copy(out=dbgp[:, D:2 * D], in_=c_bpb[:, 0:1024]), inc="sv")
                        r.dma("gpsimd", dbg1[:, 2 * D:4 * D], dbgp[:], waits=[("sv", vdp)])
                    gp_ = r.dma("gpsimd", psp[m], pbf[:], waits=[("ss", sp_)])
                    relq["pb"] = gp_
                mph = None
                for kk in range(8):
                    for gg in range(4):
                        mph = r.op("tensor", lambda e, k=kk, g2=gg: e.matmul(
                            p_p[0:16, g2 * 512:(g2 + 1) * 512],
                            xn1T_halo[:, k * 16:(k + 1) * 16],
                            w_p[:, k * DEXP + g2 * 512:k * DEXP + (g2 + 1) * 512],
                            start=(k == 0), stop=(k == 7)),
                            waits=[("sv", r.cnt["sv"]), ("ss", r.cnt["ss"]), ("qg", relq["pb"])] if (kk == 0 and gg == 0) else [],
                            inc="st" if (kk == 7 and gg == 3) else None)
                r.op("vector", lambda e: e.tensor_add(out=pstg[0:16, 0:1024], in0=p_p[0:16, 0:1024], in1=c_bpb[0:16, 0:1024]), waits=[("st", mph)], big=True, inc="sv")
                vph = r.op("vector", lambda e: e.tensor_add(out=pstg[0:16, 1024:2048], in0=p_p[0:16, 1024:2048], in1=c_bpb[0:16, 1024:2048]), big=True, inc="sv")
                r.op("scalar", lambda e: e.activation(out=phalo[:], in_=pstg[0:16, :], func=AF.Silu), waits=[("sv", vph)], inc="ss")

            if dbg and last == 2:
                r.dma("gpsimd", dbg_pos[:, 0:NT * H], pos_T[:, 0:NT * H], waits=[("ss", r.cnt["ss"]), ("sv", r.cnt["sv"])])
                if not os.environ.get("DK_NOHALODUMP"):
                    r.dma("gpsimd", dbg_halo[0:16, :], xn1_halo_t[:], waits=[])
                    r.dma("gpsimd", dbg_halo[32:48, :], x1_halo_t[:], waits=[])
                gg_ = r.op("vector", lambda e: e.tensor_copy(out=stats[:, 0:16], in_=qhalo[:, 0:16]), inc="sv")
                r.dma("gpsimd", dbg_qh[:, 0:16], stats[:, 0:16], waits=[("sv", gg_)])

        QG_P2 = r.cnt["qg"]
        BAR_P2 = r.phase_barrier() if last >= 3 else []
        if last >= 3:
            with ExitStack() as p3s:
                def sb3(name, shape, dtype):
                    return p3s.enter_context(nc.sbuf_tensor(name, shape, dtype))
                w_o = sb3("w_o", [128, 16 * D], BF16)
                c_lnow = sb3("c_lnow", [128, D], F32)
                c_lnob = sb3("c_lnob", [128, D], F32)
                qt = [sb3(f"qt{i}", [128, H * 128], BF16) for i in range(2)]
                kb = [sb3(f"kb{i}", [128, H * KC], BF16) for i in range(2)]
                vb = [sb3(f"vb{i}", [128, 4 * 2064], BF16) for i in range(2)]
                pkb = [sb3(f"pkb{i}", [H, KC], BF16) for i in range(2)]
                c_hohb = sb3("c_hohb", [16, H * 128], BF16)
                spb = [sb3(f"spb{i}", [128, DEXP], BF16) for i in range(2)]
                x1o = [sb3(f"x1o{i}", [128, D], F32) for i in range(2)]
                s0 = sb3("s0", [128, KC], F32)
                p0 = sb3("p0", [128, KC], BF16)
                pt_sb = sb3("pt_sb", [128, KC], BF16)
                otile = sb3("otile", [128, DEXP], BF16)
                ztile = sb3("ztile", [128, DEXP], BF16)
                o_acc = sb3("o_acc", [128, DEXP], F32)
                l_acc = sb3("l_acc", [128, H], F32)
                linv = sb3("linv", [128, 2 * H], F32)
                outt = sb3("outt", [128, D], F32)
                x2t = sb3("x2t", [128, D], F32)
                x1pb = sb3("x1pb", [128, D], F32)

                p_sc = [p3s.enter_context(nc.psum_tensor(f"p3_sc{i}", [128, 512], F32)) for i in range(2)]
                p_pt = p3s.enter_context(nc.psum_tensor("p3_pt", [128, 512], BF16))
                p_o = p3s.enter_context(nc.psum_tensor("p3_o", [128, 2048], F32))
                p_l = p3s.enter_context(nc.psum_tensor("p3_l", [128, 16], F32))

                for k in range(16):
                    r.dma("gpsimd", w_o[:, k * D:(k + 1) * D], wout[k * 128:(k + 1) * 128, :], waits=BAR_P2 if k == 0 else [])
                r.dma("sync", c_lnow[:], lnow_bc, waits=BAR_P2)
                r.dma("sync", c_lnob[:], lnob_bc)
                r.dma("sync", c_hohb[:], hohb)
                WPO = r.cnt["qg"]
                QSO = r.cnt["qs"]

                rel3 = {"qt": [0, 0], "kb": [0, 0], "vb": [0, 0], "spb": [0, 0],
                        "pk": [0, 0], "oacc": 0,
                        "x1o": [0, 0], "sc": [0, 0], "pt": 0, "p0": 0, "s0": 0,
                        "o": 0, "l": 0, "ot": 0, "zt": 0, "x2": 0}
                SS_P2 = r.cnt["ss"]
                SV_P2 = r.cnt["sv"]

                def attn_tile(t, halo):
                    nch = (t + 1) if not halo else NCH
                    P = 16 if halo else 128
                    tb = t % 2
                    if not halo:
                        vbb = r.op("vector", lambda e, t_=t: e.tensor_scalar(
                            out=bias16[:, 0:H], in0=pos_own[:, t_ * H:(t_ + 1) * H],
                            scalar1=-1.0, scalar2=-KCONST, op0=ALU.mult, op1=ALU.add),
                            waits=[("ss", rel3["p0"])], inc="sv")
                    else:
                        vbb = r.op("vector", lambda e: e.tensor_scalar(
                            out=bias16h[:, 0:H], in0=pos_haloT[:],
                            scalar1=-1.0, scalar2=-KCONST, op0=ALU.mult, op1=ALU.add),
                            waits=[("ss", rel3["p0"]), ("qg", QG_P2)], inc="sv")
                    if not halo:
                        lq = r.dma("sync", qt[tb][:], qsp[t], waits=[("st", rel3["qt"][tb]), ("qg", QG_P2)] + (BAR_P2 if t == 0 else []))
                        lsp = r.dma("sync", spb[tb][:], psp[t], waits=[("sv", rel3["spb"][tb])])
                        lx1 = r.dma("sync", x1o[tb][:], x1osp[t * 128:(t + 1) * 128, :], waits=[("sv", rel3["x1o"][tb])])
                    for c in range(nch):
                        cb = c % 2
                        lk = r.dma("sync", kb[cb][:], ksp[c], waits=[("st", rel3["kb"][cb]), ("qg", QG_P2)])
                        for tl4 in range(4):
                            lv = r.dma("sync", vb[cb][:, tl4 * 2064:(tl4 + 1) * 2064], vsp[c][tl4],
                                       waits=[("st", rel3["vb"][cb])] if tl4 == 0 else [])
                        lpk = r.dma("sync", pkb[cb][:], posk_sp[c], waits=[("st", rel3["pk"][cb])])
                        if halo:
                            lm = r.dma("sync", hmaskb[:], halo_mask[:, c * KC:(c + 1) * KC], waits=[("ss", rel3["s0"])])
                            lm = r.fence("sync")
                        for h in range(H):
                            sc = p_sc[h % 2]
                            qap = qt[tb][:, h * 128:(h + 1) * 128] if not halo else qhalo[:, h * 16:(h + 1) * 16]
                            mmq = r.op("tensor", lambda e, sc=sc, qap=qap, cb=cb, h_=h, P_=P: e.matmul(
                                sc[0:P_, :], qap, kb[cb][:, h_ * KC:(h_ + 1) * KC], start=True, stop=False),
                                waits=[("qs", lk), ("ss", rel3["sc"][h % 2]), ("sv", vbb if (c == 0 and h == 0) else 0),
                                       ("qs", (lq if not halo else 0))],
                                inc=None)
                            mmq = r.op("tensor", lambda e, sc=sc, cb=cb, h_=h, P_=P: e.matmul(
                                sc[0:P_, :], c_hohb[:, h_ * 128:h_ * 128 + P_], pkb[cb][:],
                                start=False, stop=True),
                                waits=[("qs", lpk)] if h == 0 else [],
                                inc="st")
                            bcol = (bias16 if not halo else bias16h)[0:P, h:h + 1]
                            if (not halo) and c == t:
                                vdm = r.op("vector", lambda e, sc=sc: e.scalar_tensor_tensor(
                                    out=s0[:], in0=sc[:], scalar=1.0, in1=c_diag[:], op0=ALU.mult, op1=ALU.add),
                                    waits=[("st", mmq), ("ss", rel3["s0"])], inc="sv")
                                se_ = r.op("scalar", lambda e, bcol=bcol: e.activation(
                                    out=p0[:], in_=s0[:], func=AF.Exp, bias=bcol),
                                    waits=[("sv", vdm), ("st", rel3["pt"])], inc="ss")
                                rel3["s0"] = se_
                            elif halo:
                                vdm = r.op("vector", lambda e, sc=sc, P_=P: e.scalar_tensor_tensor(
                                    out=s0[0:P_, :], in0=sc[0:P_, :], scalar=1.0, in1=hmaskb[:], op0=ALU.mult, op1=ALU.add),
                                    waits=[("st", mmq), ("qs", lm), ("ss", rel3["s0"])], inc="sv")
                                se_ = r.op("scalar", lambda e, bcol=bcol, P_=P: e.activation(
                                    out=p0[0:P_, :], in_=s0[0:P_, :], func=AF.Exp, bias=bcol),
                                    waits=[("sv", vdm), ("st", rel3["pt"])], inc="ss")
                                rel3["s0"] = se_
                            else:
                                se_ = r.op("scalar", lambda e, sc=sc, bcol=bcol: e.activation(
                                    out=p0[:], in_=sc[:], func=AF.Exp, bias=bcol),
                                    waits=[("st", mmq), ("st", rel3["pt"])], inc="ss")
                            rel3["sc"][h % 2] = se_
                            tp = None
                            for blk in range(4):
                                tp = r.op("tensor", lambda e, b_=blk, P_=P: e.transpose(
                                    p_pt[:, b_ * 128:b_ * 128 + P_], p0[0:P_, b_ * 128:(b_ + 1) * 128], c_eyebf[0:P_, 0:P_]),
                                    waits=[("ss", se_), ("sv", rel3["o"])] if blk == 0 else [],
                                    inc="st" if blk == 3 else None)
                            vpt = r.op("vector", lambda e: e.tensor_copy(out=pt_sb[:], in_=p_pt[:]), waits=[("st", tp)], inc="sv")
                            if dbg and last == 3 and (not halo) and t == DBGT and c == DBGC and h == 0:
                                dbg3 = p3s.enter_context(nc.sbuf_tensor("dbg3", [128, 2 * KC], F32))
                                r.op("vector", lambda e: e.tensor_copy(out=dbg3[:, 0:KC], in_=p0[:]), inc="sv")
                                vd3 = r.op("vector", lambda e: e.tensor_copy(out=dbg3[:, KC:2 * KC], in_=s0[:]), inc="sv")
                                r.dma("gpsimd", dbg1[:, 0:2 * KC], dbg3[:], waits=[("sv", vd3)])
                                fz0 = r.fence("gpsimd")
                                vd3b = r.op("vector", lambda e, cb=cb: e.tensor_copy(out=dbg3[0:16, 0:KC], in_=pkb[cb][:]), waits=[("qg", fz0)], inc="sv")
                                r.dma("gpsimd", dbg1[0:16, 4096:4096 + KC], dbg3[0:16, 0:KC], waits=[("sv", vd3b)])
                                r.fence("gpsimd")
                            rel3["pt"] = r.cnt["st"]
                            mav = None
                            for blk in range(4):
                                mav = r.op("tensor", lambda e, b_=blk, cb=cb, h_=h, P_=P: e.matmul(
                                    p_o[0:P_, h_ * 128:(h_ + 1) * 128], pt_sb[:, b_ * 128:b_ * 128 + P_],
                                    vb[cb][:, b_ * 2064 + h_ * 129:b_ * 2064 + h_ * 129 + 128],
                                    start=(b_ == 0), stop=(b_ == 3)),
                                    waits=[("sv", vpt), ("qs", lv), ("sv", rel3["oacc"] if h == 0 else 0)] if blk == 0 else [],
                                    inc=None)
                                mav = r.op("tensor", lambda e, b_=blk, cb=cb, h_=h, P_=P: e.matmul(
                                    p_l[0:P_, h_:h_ + 1], pt_sb[:, b_ * 128:b_ * 128 + P_],
                                    vb[cb][:, b_ * 2064 + h_ * 129 + 128:b_ * 2064 + h_ * 129 + 129],
                                    start=(b_ == 0), stop=(b_ == 3)),
                                    inc="st" if (blk == 3) else None)
                            rel3["o"] = vpt
                        MAVC = r.cnt["st"]
                        c0_ = (c == 0)
                        vo1 = r.op("vector", lambda e, P_=P, c0=c0_: (
                            e.tensor_copy(out=o_acc[0:P_, 0:1024], in_=p_o[0:P_, 0:1024]) if c0 else
                            e.tensor_add(out=o_acc[0:P_, 0:1024], in0=o_acc[0:P_, 0:1024], in1=p_o[0:P_, 0:1024])),
                            waits=[("st", MAVC)] + ([("ss", rel3["ot"])] if c0_ else []), big=True, inc="sv")
                        vo2 = r.op("vector", lambda e, P_=P, c0=c0_: (
                            e.tensor_copy(out=o_acc[0:P_, 1024:2048], in_=p_o[0:P_, 1024:2048]) if c0 else
                            e.tensor_add(out=o_acc[0:P_, 1024:2048], in0=o_acc[0:P_, 1024:2048], in1=p_o[0:P_, 1024:2048])),
                            big=True, inc="sv")
                        vla = r.op("vector", lambda e, P_=P, c0=c0_: (
                            e.tensor_copy(out=l_acc[0:P_, :], in_=p_l[0:P_, :]) if c0 else
                            e.tensor_add(out=l_acc[0:P_, :], in0=l_acc[0:P_, :], in1=p_l[0:P_, :])),
                            inc="sv")
                        rel3["oacc"] = vla
                        rel3["kb"][cb] = r.cnt["st"]
                        rel3["vb"][cb] = r.cnt["st"]
                        rel3["pk"][cb] = r.cnt["st"]
                    MAV = r.cnt["st"]
                    vle = r.op("vector", lambda e, P_=P: e.tensor_scalar(out=linv[0:P_, 0:H], in0=l_acc[0:P_, :], scalar1=1e-30, scalar2=None, op0=ALU.add, op1=ALU.bypass),
                               waits=[("sv", rel3["oacc"])], inc="sv")
                    vli = r.op("vector", lambda e, P_=P: e.reciprocal(out=linv[0:P_, H:2 * H], in_=linv[0:P_, 0:H]), waits=[("sv", vle)], inc="sv")
                    so = None
                    for h in range(H):
                        so = r.op("scalar", lambda e, h_=h, P_=P: e.activation(
                            out=otile[0:P_, h_ * 128:(h_ + 1) * 128], in_=o_acc[0:P_, h_ * 128:(h_ + 1) * 128],
                            func=AF.Identity, scale=linv[0:P_, H + h_:H + h_ + 1]),
                            waits=[("sv", vli)] if h == 0 else [], inc="ss" if h == H - 1 else None)
                    rel3["ot"] = so
                    if dbg and last == 3 and (not halo) and t == DBGT:
                        vd4 = r.op("vector", lambda e: e.tensor_copy(out=ztile[:], in_=otile[:]), waits=[("ss", so)], inc="sv")
                        r.dma("gpsimd", dbg1[:, 2 * KC:2 * KC + DEXP], ztile[:], waits=[("sv", vd4)])
                        vd5 = r.op("vector", lambda e: e.tensor_copy(out=s0[:, 0:H], in_=linv[:, 0:H]), inc="sv")
                        vd5 = r.op("vector", lambda e: e.tensor_copy(out=s0[:, H:2 * H], in_=bias16[:]), inc="sv")
                        r.dma("gpsimd", dbg1[:, 2 * KC + DEXP:2 * KC + DEXP + H + NCH * H], s0[:, 0:H + NCH * H], waits=[("sv", vd5)])
                        fz = r.fence("gpsimd")
                        rel3["zt"] = rel3["zt"]
                    pin = spb[tb] if not halo else phalo
                    vz = r.op("vector", lambda e, pin=pin, P_=P: e.tensor_mul(out=ztile[0:P_, :], in0=pin[0:P_, :] if P_ == 128 else pin[:], in1=otile[0:P_, :]),
                              waits=[("ss", so), ("qs", (lsp if not halo else 0)), ("st", rel3["zt"])], inc="sv")
                    szT = None
                    for rnd in range(4):
                        tz = None
                        for blk in range(4):
                            kk = rnd * 4 + blk
                            tz = r.op("tensor", lambda e, k_=kk, b_=blk, P_=P: e.transpose(
                                p_pt[:, b_ * 128:b_ * 128 + P_], ztile[0:P_, k_ * 128:(k_ + 1) * 128], c_eyebf[0:P_, 0:P_]),
                                waits=[("sv", vz)] if kk == 0 else ([("ss", szT)] if blk == 0 else [])  ,
                                inc="st" if blk == 3 else None)
                        szT = r.op("scalar", lambda e: e.copy(out=pt_sb[:, 0:512], in_=p_pt[:]), waits=[("st", tz)], inc="ss")
                        szT = r.op("scalar", lambda e, r_=rnd: e.copy(out=otile[:, r_ * 512:(r_ + 1) * 512], in_=pt_sb[:, 0:512]), inc="ss")
                    rel3["zt"] = r.cnt["st"]
                    mo = None
                    for kk in range(16):
                        for gg in range(2):
                            mo = r.op("tensor", lambda e, k=kk, g2=gg, P_=P: e.matmul(
                                p_sc[g2][0:P_, :],
                                otile[:, k * 128:k * 128 + P_],
                                w_o[:, k * D + g2 * 512:k * D + (g2 + 1) * 512],
                                start=(k == 0), stop=(k == 15)),
                                waits=[("ss", szT), ("qg", WPO), ("ss", rel3["sc"][0]), ("ss", rel3["sc"][1])] if (kk == 0 and gg == 0) else [],
                                inc="st" if (kk == 15 and gg == 1) else None)
                    vs1 = r.op("vector", lambda e, P_=P: e.reduce_sum(stats[0:P_, 0:1], p_sc[0][0:P_, :], axis=AX.X), waits=[("st", mo)], inc="sv")
                    vs2 = r.op("vector", lambda e, P_=P: e.reduce_sum(stats[0:P_, 1:2], p_sc[1][0:P_, :], axis=AX.X), inc="sv")
                    vc1 = r.op("vector", lambda e, P_=P: e.tensor_copy(out=outt[0:P_, 0:512], in_=p_sc[0][0:P_, :]), big=True, inc="sv")
                    vc2 = r.op("vector", lambda e, P_=P: e.tensor_copy(out=outt[0:P_, 512:1024], in_=p_sc[1][0:P_, :]), big=True, inc="sv")
                    ssq = r.op("scalar", lambda e, P_=P: e.activation(out=x2t[0:P_, :], in_=outt[0:P_, :], func=AF.Square, accum_out=stats[0:P_, 2:3]),
                               waits=[("sv", vc2)], inc="ss")
                    rel3["sc"] = [r.cnt["ss"], r.cnt["ss"]]
                    va = r.op("vector", lambda e, P_=P: e.tensor_add(out=stats[0:P_, 0:1], in0=stats[0:P_, 0:1], in1=stats[0:P_, 1:2]), waits=[("sv", vs2)], inc="sv")
                    vm1 = r.op("vector", lambda e, P_=P: ts_mul(e, out=stats[0:P_, 4:5], in0=stats[0:P_, 0:1], scalar1=1.0 / D), waits=[("sv", va)], inc="sv")
                    vm2 = r.op("vector", lambda e, P_=P: ts_mul(e, out=stats[0:P_, 5:6], in0=stats[0:P_, 2:3], scalar1=1.0 / D), waits=[("ss", ssq)], inc="sv")
                    vm3 = r.op("vector", lambda e, P_=P: e.tensor_mul(out=stats[0:P_, 16:17], in0=stats[0:P_, 4:5], in1=stats[0:P_, 4:5]), waits=[("sv", vm1)], inc="sv")
                    vm4 = r.op("vector", lambda e, P_=P: e.tensor_sub(out=stats[0:P_, 6:7], in0=stats[0:P_, 5:6], in1=stats[0:P_, 16:17]), waits=[("sv", vm3), ("sv", vm2)], inc="sv")
                    vm5 = r.op("vector", lambda e, P_=P: e.tensor_scalar(out=stats[0:P_, 14:15], in0=stats[0:P_, 6:7], scalar1=EPS, scalar2=None, op0=ALU.add, op1=ALU.bypass), waits=[("sv", vm4)], inc="sv")
                    vm6 = r.op("vector", lambda e, P_=P: e.reciprocal(out=stats[0:P_, 7:8], in_=stats[0:P_, 14:15]), waits=[("sv", vm5)], inc="sv")
                    sq2_ = r.op("scalar", lambda e, P_=P: e.activation(out=stats[0:P_, 8:9], in_=stats[0:P_, 7:8], func=AF.Sqrt), waits=[("sv", vm6)], inc="ss")
                    vbx = r.op("vector", lambda e, P_=P: e.tensor_scalar(out=stats[0:P_, 9:10], in0=stats[0:P_, 4:5], scalar1=stats[0:P_, 8:9], scalar2=-1.0, op0=ALU.mult, op1=ALU.mult), waits=[("ss", sq2_)], inc="sv")
                    sap = r.op("scalar", lambda e, P_=P: e.activation(out=x2t[0:P_, :], in_=outt[0:P_, :], func=AF.Identity, bias=stats[0:P_, 9:10], scale=stats[0:P_, 8:9]),
                               waits=[("sv", vbx)], inc="ss")
                    if not halo:
                        vpb = r.op("vector", lambda e: e.tensor_add(out=x1pb[:], in0=x1o[tb][:], in1=c_lnob[:]), waits=[("qs", lx1), ("ss", sap)], inc="sv")
                        vw2 = r.op("vector", lambda e: e.tensor_mul(out=x2t[:], in0=x2t[:], in1=c_lnow[:]), inc="sv")
                        vx2 = r.op("vector", lambda e: e.tensor_add(out=x2t[:], in0=x2t[:], in1=x1pb[:]), inc="sv")
                        rel3["x1o"][tb] = vx2
                        gx2 = r.dma("gpsimd", x2sp[t * 128:(t + 1) * 128, :], x2t[:], waits=[("sv", vx2)])
                        rel3["spb"][tb] = vz
                        rel3["qt"][tb] = MAV
                        rel3["x2"] = gx2
                    else:
                        vpb = r.op("vector", lambda e: e.tensor_add(out=x1pb[0:16, :], in0=x1_halo[:, :], in1=c_lnob[0:16, :]), waits=[("ss", sap)], inc="sv")
                        vw2 = r.op("vector", lambda e: e.tensor_mul(out=x2t[0:16, :], in0=x2t[0:16, :], in1=c_lnow[0:16, :]), inc="sv")
                        vx2 = r.op("vector", lambda e: e.tensor_add(out=x2t[0:16, :], in0=x2t[0:16, :], in1=x1pb[0:16, :]), inc="sv")
                        gx2 = r.dma("gpsimd", x2_halo[:, :], x2t[0:16, :], waits=[("sv", vx2)])
                        rel3["x2"] = gx2
                    rel3["p0"] = r.cnt["ss"]

                for t in range(TPC):
                    attn_tile(t, halo=False)
                attn_tile(NCH - 1, halo=True)

        QG_P3 = r.cnt["qg"]
        BAR_P3 = r.phase_barrier() if last >= 4 else []
        if last >= 4:
            with ExitStack() as p4s:
                def sb4(name, shape, dtype):
                    return p4s.enter_context(nc.sbuf_tensor(name, shape, dtype))
                w_r = sb4("w4_r", [128, 8 * D], BF16)
                w_k = sb4("w4_k", [128, 8 * D], BF16)
                w_v = sb4("w4_v", [128, 8 * D], BF16)
                x2b = [sb4(f"x2b{i}", [128, D], F32) for i in range(2)]
                u3 = sb4("u3", [128, D], F32)
                u3p = sb4("u3p", [128, D], F32)
                d4 = sb4("d4", [128, D], F32)
                sq4 = sb4("sq4", [128, D], F32)
                xrT4 = sb4("xrT4", [128, D], BF16)
                xkT4 = sb4("xkT4", [128, D], BF16)
                rga4 = sb4("rga4", [128, D], F32)
                upT4 = sb4("upT4", [128, D], F32)
                karel4 = sb4("karel4", [128, 128], F32)
                kaT4 = sb4("kaT4", [128, D], BF16)
                x3t = [sb4(f"x3t{i}", [128, D], F32) for i in range(2)]

                p_tp = p4s.enter_context(nc.psum_tensor("p4_tp", [128, 1024], F32))
                p_tp2 = p4s.enter_context(nc.psum_tensor("p4_tp2", [128, 1024], F32))
                p_mm = p4s.enter_context(nc.psum_tensor("p4_mm", [128, 1024], F32))
                p_kb = p4s.enter_context(nc.psum_tensor("p4_kb", [128, 128], F32))

                for k in range(8):
                    r.dma("gpsimd", w_r[:, k * D:(k + 1) * D], wr[k * 128:(k + 1) * 128, :], waits=BAR_P3 if k == 0 else [])
                    r.dma("gpsimd", w_k[:, k * D:(k + 1) * D], wk[k * 128:(k + 1) * 128, :])
                    r.dma("gpsimd", w_v[:, k * D:(k + 1) * D], wv[k * 128:(k + 1) * 128, :])
                WP4 = r.cnt["qg"]

                hx = [("qg", r.cnt["qg"]), ("sv", r.cnt["sv"]), ("ss", r.cnt["ss"])]
                sqh, vb8, _ = ln_plain(x2_halo[:, :], u3p[0:16, :], st16, [("sv", r.cnt["sv"]), ("ss", r.cnt["ss"]), ("qg", r.cnt["qg"])])
                vhv = r.op("vector", lambda e: e.tensor_mul(out=st16[:, 18:19], in0=st16[:, 4:5], in1=c_hvald[:]), waits=[("sv", vb8)], inc="sv")
                vhb = r.op("vector", lambda e: e.tensor_mul(out=st16[:, 19:20], in0=st16[:, 5:6], in1=c_hvald[:]), waits=[("sv", vhv)], inc="sv")
                suh = r.op("scalar", lambda e: e.activation(out=u3_halo[:, :], in_=x2_halo[:, :], func=AF.Identity, bias=st16[:, 19:20], scale=st16[:, 18:19]),
                           waits=[("sv", vhb)], inc="ss")

                rel4 = {"x2": [0, 0], "x3": [0, 0]}
                for m in range(TPC):
                    b = m % 2
                    xv, x3v = x2b[b], x3t[b]
                    ld = r.dma("sync", xv[:], x2sp[m * 128:(m + 1) * 128, :],
                               waits=([("sv", rel4["x2"][b])] if m >= 2 else [("ss", suh)] + BAR_P3) + [("qg", QG_P3)])
                    sr, vb_, _v7 = ln_plain(xv[:], sq4[:], stats, [("qs", ld)], extra_s_wait=[("sv", r.cnt["sv"])])
                    su = r.op("scalar", lambda e, xv=xv: e.activation(out=u3[:], in_=xv[:], func=AF.Identity, bias=stats[:, 5:6], scale=stats[:, 4:5]),
                              waits=[("sv", vb_), ("qg", rel4.get("u", 0)), ("sv", rel4.get("ud", 0))], inc="ss")
                    g1 = r.dma("gpsimd", u3p[0:1, :], u3_halo[m:m + 1, :], waits=[("ss", su), ("sv", r.cnt["sv"]), ("st", r.cnt["st"])])
                    g2 = r.dma("gpsimd", u3p[1:128, :], u3[0:127, :])
                    g2 = r.fence("gpsimd")
                    rel4["u"] = g2
                    vd = r.op("vector", lambda e: e.tensor_sub(out=d4[:], in0=u3[:], in1=u3p[:]),
                              waits=[("qg", g2), ("st", r.cnt["st"])], inc="sv")
                    rel4["ud"] = vd
                    t1 = t2 = None
                    for kk in range(8):
                        t1 = r.op("tensor", lambda e, k=kk: e.transpose(p_tp[:, k * 128:(k + 1) * 128], d4[:, k * 128:(k + 1) * 128], c_eye32[:]),
                                  waits=[("sv", vd)] if kk == 0 else [], inc="st" if kk == 7 else None)
                    for kk in range(8):
                        t2 = r.op("tensor", lambda e, k=kk: e.transpose(p_tp2[:, k * 128:(k + 1) * 128], u3p[:, k * 128:(k + 1) * 128], c_eye32[:]),
                                  inc="st" if kk == 7 else None)
                    ve = r.op("vector", lambda e: e.tensor_copy(out=upT4[:], in_=p_tp2[:]), waits=[("st", t2)], inc="sv")
                    vxr = None
                    for kk in range(8):
                        vxr = r.op("vector", lambda e, k=kk: e.scalar_tensor_tensor(out=xrT4[:, k * 128:(k + 1) * 128], in0=p_tp[:, k * 128:(k + 1) * 128], scalar=c_murc[:, k:k + 1], in1=upT4[:, k * 128:(k + 1) * 128], op0=ALU.mult, op1=ALU.add),
                                  waits=[("st", t1)] if kk == 0 else [], inc="sv" if kk == 7 else None)
                    vxk = None
                    for kk in range(8):
                        vxk = r.op("vector", lambda e, k=kk: e.scalar_tensor_tensor(out=xkT4[:, k * 128:(k + 1) * 128], in0=p_tp[:, k * 128:(k + 1) * 128], scalar=c_mukc[:, k:k + 1], in1=upT4[:, k * 128:(k + 1) * 128], op0=ALU.mult, op1=ALU.add),
                                  inc="sv" if kk == 7 else None)
                    mmr = None
                    for kk in range(8):
                        for g in range(2):
                            mmr = r.op("tensor", lambda e, k=kk, gg=g: e.matmul(
                                p_mm[:, gg * 512:(gg + 1) * 512], xrT4[:, k * 128:(k + 1) * 128],
                                w_r[:, k * D + gg * 512:k * D + (gg + 1) * 512],
                                start=(k == 0), stop=(k == 7)),
                                waits=([("sv", vxr), ("qg", WP4), ("ss", r.cnt["ss"])] if (kk == 0 and g == 0) else []),
                                inc="st" if (kk == 7 and g == 1) else None)
                    sg = r.op("scalar", lambda e: e.activation(out=rga4[:], in_=p_mm[:], func=AF.Sigmoid),
                              waits=[("st", mmr), ("sv", vxk)], inc="ss")
                    for blk in range(8):
                        mk_ = None
                        for kk in range(8):
                            mk_ = r.op("tensor", lambda e, k=kk, b2=blk: e.matmul(
                                p_kb[:], w_k[:, k * D + b2 * 128:k * D + (b2 + 1) * 128],
                                xkT4[:, k * 128:(k + 1) * 128],
                                start=(k == 0), stop=(k == 7)),
                                waits=([("sv", vxk), ("ss", sg)] if (kk == 0 and blk == 0) else ([("sv", r.cnt["sv"])] if kk == 0 else [])),
                                inc="st" if kk == 7 else None)
                        sre = r.op("scalar", lambda e: e.activation(out=karel4[:], in_=p_kb[:], func=AF.Relu),
                                   waits=[("st", mk_), ("sv", r.cnt["sv"])], inc="ss")
                        r.op("vector", lambda e, b2=blk: e.tensor_mul(out=kaT4[:, b2 * 128:(b2 + 1) * 128], in0=karel4[:], in1=karel4[:]),
                             waits=[("ss", sre)], inc="sv")
                    vka = r.cnt["sv"]
                    mmv = None
                    for kk in range(8):
                        for g in range(2):
                            mmv = r.op("tensor", lambda e, k=kk, gg=g: e.matmul(
                                p_mm[:, gg * 512:(gg + 1) * 512], kaT4[:, k * 128:(k + 1) * 128],
                                w_v[:, k * D + gg * 512:k * D + (gg + 1) * 512],
                                start=(k == 0), stop=(k == 7)),
                                waits=[("sv", vka)] if (kk == 0 and g == 0) else [],
                                inc="st" if (kk == 7 and g == 1) else None)
                    vm = r.op("vector", lambda e: e.tensor_mul(out=sq4[:], in0=rga4[:], in1=p_mm[:]),
                              waits=[("st", mmv), ("ss", r.cnt["ss"])], inc="sv")
                    vx3 = r.op("vector", lambda e, xv=xv, x3v=x3v: e.scalar_tensor_tensor(out=x3v[:], in0=sq4[:], scalar=0.5, in1=xv[:], op0=ALU.mult, op1=ALU.add),
                               waits=[("qg", rel4["x3"][b])], inc="sv")
                    rel4["x2"][b] = vx3
                    go = r.dma("gpsimd", out[m * 128:(m + 1) * 128, :], x3v[:], waits=[("sv", vx3)])
                    rel4["x3"][b] = go

        QGE, QSE = r.cnt["qg"], r.cnt["qs"]
        r.op("sync", lambda e: e.nop(), waits=[("qg", QGE), ("qs", QSE)], inc=None)

        blk = es.enter_context(nc.Block())
        r.emit(blk)
    return nc


def stage_inputs(inputs, core):
    b = core // 4
    c = core % 4
    xfull = np.ascontiguousarray(np.asarray(inputs["L0_attn_x"], np.float32)[b])
    W_in = np.asarray(inputs["L0_attn_W_in"], np.float32)
    b_in = np.asarray(inputs["L0_attn_b_in"], np.float32)
    hsel_idx = (c - 1) % 4
    hselm = np.eye(4, dtype=np.float32)[hsel_idx].reshape(1, 4).copy()
    m = {
        "x": xfull,
        "wr": np.ascontiguousarray(np.asarray(inputs["L0_ffn_W_r"], np.float32).T),
        "wk": np.ascontiguousarray(np.asarray(inputs["L0_ffn_W_k"], np.float32).T),
        "wv": np.ascontiguousarray(np.asarray(inputs["L0_ffn_W_v"], np.float32).T),
        "mur_c": np.ascontiguousarray(np.asarray(inputs["L0_ffn_mu_r"], np.float32).reshape(8, 128).T),
        "muk_c": np.ascontiguousarray(np.asarray(inputs["L0_ffn_mu_k"], np.float32).reshape(8, 128).T),
        "wq": np.ascontiguousarray(W_in[0:2048].T),
        "wka": np.ascontiguousarray(W_in[2048:4096].T),
        "wva": np.ascontiguousarray(W_in[4096:6144].T),
        "wp": np.ascontiguousarray(W_in[6144:8192].T),
        "wy": np.ascontiguousarray(W_in[8192:8208].T),
        "bq_cols": np.ascontiguousarray(b_in[0:2048].reshape(16, 128).T),
        "bk_cols": np.ascontiguousarray(b_in[2048:4096].reshape(16, 128).T),
        "bv_bc": np.broadcast_to(b_in[4096:6144], (128, 2048)).copy(),
        "bp_bc": np.broadcast_to(b_in[6144:8192], (128, 2048)).copy(),
        "by_bc": np.broadcast_to(b_in[8192:8208], (128, 16)).copy(),
        "lnw_bc": np.broadcast_to(np.asarray(inputs["L0_attn_in_ln_w"], np.float32), (128, 1024)).copy(),
        "lnb_bc": np.broadcast_to(np.asarray(inputs["L0_attn_in_ln_b"], np.float32), (128, 1024)).copy(),
        "lnow_bc": np.broadcast_to(np.asarray(inputs["L0_attn_out_ln_w"], np.float32), (128, 1024)).copy(),
        "lnob_bc": np.broadcast_to(np.asarray(inputs["L0_attn_out_ln_b"], np.float32), (128, 1024)).copy(),
        "smear": np.asarray(inputs["L0_attn_smear_factor"], np.float32).reshape(1, 16),
        "logsc": np.asarray(inputs["L0_attn_log_scale"], np.float32).reshape(1, 16),
        "wout": np.ascontiguousarray(np.asarray(inputs["L0_attn_W_out"], np.float32).T),
        "triu": np.triu(np.ones((128, 128), np.float32)),
        "eye32": np.eye(128, dtype=np.float32),
        "eyebf": np.eye(128, dtype=np.float32).astype(ml_dtypes.bfloat16),
        "ones_row": np.ones((1, 128), np.float32),
        "sel4": np.broadcast_to(np.eye(4, dtype=np.float32)[c], (128, 4)).copy(),
        "e127": np.eye(128, dtype=np.float32)[:, 127:128].copy(),
        "hselm": hselm,
        "hohb": np.repeat(np.eye(16, dtype=np.float32), 128, axis=1).astype(ml_dtypes.bfloat16),
        "halo_valid": np.ones((16, 1), np.float32) if c != 0 else np.concatenate([np.zeros((1, 1), np.float32), np.ones((15, 1), np.float32)], 0),
    }
    i = np.arange(128)[:, None]
    j = np.arange(KC)[None, :]
    m["diag_mask"] = np.where(j <= 128 * c + i, 0.0, NEG).astype(np.float32)
    mm_ = np.arange(16)[:, None]
    keyg = np.arange(NCH * KC)[None, :]
    tok = 512 * mm_ + 128 * c - 1
    m["halo_mask"] = np.where(keyg <= tok, 0.0, NEG).astype(np.float32)
    return m


def kernel(**inputs):
    nc = build_program(debug_phase=0)
    in_maps = [stage_inputs(inputs, core) for core in range(8)]
    res = run_bass_kernel_spmd(nc, in_maps, core_ids=list(range(8)))
    outp = np.zeros((2, S, D), np.float32)
    for core in range(8):
        b, c = core // 4, core % 4
        o = np.asarray(res.results[core]["out"], np.float32)
        for mt in range(TPC):
            outp[b, 512 * mt + 128 * c:512 * mt + 128 * c + 128, :] = o[mt * 128:(mt + 1) * 128, :]
    return outp


if __name__ == "__main__":
    print("module ok")
